# version 06 (final)
# speedup vs baseline: 1.3127x; 1.0131x over previous
import numpy as np
import ml_dtypes
import concourse.bass as bass
import concourse.mybir as mybir
from concourse.bass_utils import run_bass_kernel_spmd

F32 = mybir.dt.float32
BF16 = mybir.dt.bfloat16
BF = ml_dtypes.bfloat16
AF = mybir.ActivationFunctionType
ALU = mybir.AluOpType
AX = mybir.AxisListType

M32 = float(np.float32(3 * 2.0**22))
M52 = float(3 * 2.0**51)

NTOK, D, HF, DIN, INTER = 1024, 2048, 8192, 8512, 4096
NCV = INTER + 2 * 128
NH, HD, NST = 64, 64, 128
CH, NCH, KC = 128, 8, 16
LN_EPS, RMS_EPS, GEPS = 1e-5, 1e-8, 1e-5
STAGE = 2
DEBUG_L0 = False
SKIP_CC = False


class Prog:
    ENGINES = ("tensor", "vector", "scalar", "gpsimd", "sync")

    NDSEM = 8

    def __init__(self):
        self.ops = {e: [] for e in self.ENGINES}
        self.cnt = {e: 0 for e in self.ENGINES}
        self.dmacnt = {e: 0 for e in self.ENGINES}
        self.cccnt = 0
        self.buf = {}
        self.waited = {e: {} for e in self.ENGINES}

    def op(self, eng, fn, reads=(), writes=(), is_dma=False, is_cc=False):
        seen = {}
        def add(tok):
            s, v = tok
            if self.waited[eng].get(s, 0) >= v:
                return
            if seen.get(s, 0) < v:
                seen[s] = v
        for k in reads:
            e = self.buf.get(k)
            if e:
                for t in e[0]:
                    add(t)
        for k in writes:
            e = self.buf.get(k)
            if e:
                for t in e[0]:
                    add(t)
                for t in e[1]:
                    add(t)
        waits = sorted(seen.items())
        for s, v in waits:
            self.waited[eng][s] = v
        if is_cc:
            self.cccnt += 1
            tok, inc = ("CC", self.cccnt), ("CC", 1)
        elif is_dma:
            idx = self.dmacnt[eng]
            self.dmacnt[eng] += 1
            j = idx % self.NDSEM
            u = idx // self.NDSEM + 1
            sname = f"D_{eng}_{j}"
            if u > 1 and self.waited[eng].get(sname, 0) < (u - 1) * 16:
                if seen.get(sname, 0) < (u - 1) * 16:
                    seen[sname] = (u - 1) * 16
                    waits = sorted(seen.items())
                    self.waited[eng][sname] = (u - 1) * 16
            tok, inc = (sname, u * 16), (sname, 16)
        else:
            self.cnt[eng] += 1
            tok, inc = (f"E_{eng}", self.cnt[eng]), (f"E_{eng}", 1)
        for k in writes:
            self.buf[k] = [[tok], []]
        for k in reads:
            e = self.buf.get(k)
            if e is not None:
                e[1].append(tok)
        self.ops[eng].append((waits, fn, inc))
        return tok

    def retag(self, old_keys, new_keys):
        toks = []
        for k in old_keys:
            e = self.buf.pop(k, None)
            if e:
                toks.extend(e[0])
                toks.extend(e[1])
        best = {}
        for s, v in toks:
            best[s] = max(best.get(s, 0), v)
        toks = [(s, v) for s, v in best.items()]
        for k in new_keys:
            self.buf[k] = [list(toks), []]

    def emit(self, block, sems):
        for eng in self.ENGINES:
            oplist = self.ops[eng]
            def mk(oplist):
                def body(h):
                    for waits, fn, (s, iv) in oplist:
                        for ws, wv in waits:
                            h.wait_ge(sems[ws], wv)
                        fn(h).then_inc(sems[s], iv)
                return body
            getattr(block, eng)(mk(oplist))


def host_constants():
    G = np.zeros((NH, 32 * 128), np.float32)
    for j in range(32):
        for m in range(128):
            G[2 * j + m // 64, j * 128 + m] = 1.0
    return {
        "C_eye": np.eye(128, dtype=np.float32),
        "C_tril": np.tril(np.ones((128, 128), np.float32)).astype(BF),
        "C_triu": np.triu(np.ones((128, 128), np.float32)).astype(BF),
        "C_G": G.astype(BF),
    }


def build_program():
    import contextlib
    nc = bass.Bass("TRN2", target_bir_lowering=False)
    P = Prog()
    stack = contextlib.ExitStack()

    def din(name, shape, dt=F32):
        return nc.dram_tensor(name, list(shape), dt, kind="ExternalInput").ap()

    x_in = din("x", (NTOK, D))
    Wqk, Wv, Wu, Wo = din("Wqk", (D, D)), din("Wv", (D, D)), din("Wu", (D, D)), din("Wo", (D, D))
    b_in = din("bvec", (4, D))
    lnv = din("lnv", (2, D))
    Wg, Wu2, Wd = din("Wg", (HF, D)), din("Wu2", (HF, D)), din("Wd", (D, HF))
    Win = din("Win", (D, DIN))
    conv_w, conv_b = din("conv_w", (NCV, 4)), din("conv_b", (NCV,))
    dtb_in, A_in, D_in = din("dtb", (NH,)), din("Av", (NH,)), din("Dv", (NH,))
    nw_in = din("nw", (INTER,))
    Wout = din("Wout", (INTER, D))
    Wf, Rf = din("Wf", (D, D)), din("Rf", (D, D))
    C_eye = din("C_eye", (128, 128))
    C_tril = din("C_tril", (128, 128), BF16)
    C_triu = din("C_triu", (128, 128), BF16)
    C_G = din("C_G", (NH, 32 * 128), BF16)
    hsel = din("hsel", (128, 4))
    ssel = din("ssel", (128, 4))

    out_t = nc.dram_tensor("out", [NTOK, D], F32, kind="ExternalOutput").ap()
    if DEBUG_L0:
        dbg_g = nc.dram_tensor("dbg_g", [KC, 128, NTOK], F32, kind="ExternalOutput").ap()
        dbg_t = nc.dram_tensor("dbg_t", [KC, 128, NTOK], F32, kind="ExternalOutput").ap()
        dbg_o = nc.dram_tensor("dbg_o", [KC, 128, NTOK], F32, kind="ExternalOutput").ap()
        dbg_hi = nc.dram_tensor("dbg_hi", [KC, 128, NTOK], F32, kind="ExternalOutput").ap()
        dbg_lo = nc.dram_tensor("dbg_lo", [KC, 128, NTOK], F32, kind="ExternalOutput").ap()
        dbg_hitm = nc.dram_tensor("dbg_hitm", [8, 128, D], F32, kind="ExternalOutput").ap()
        dbg_lotm = nc.dram_tensor("dbg_lotm", [8, 128, D], F32, kind="ExternalOutput").ap()

    def dscr(name, shape, dt=F32):
        return nc.dram_tensor(name, list(shape), dt).ap()

    x1_dr = dscr("x1_dr", (NTOK, D))
    u_dr = dscr("u_dr", (KC, 128, NTOK), BF16)
    p_dr = dscr("p_dr", (HF // 128, 128, NTOK), BF16)
    wdT_dr = dscr("wdT_dr", (16, 128, HF), BF16)
    g_dr = dscr("g_dr", (KC, 128, NTOK), BF16)
    gate_dr = dscr("gate_dr", (32, 128, NTOK), BF16)
    g2_dr = dscr("g2_dr", (KC, 128, NTOK), BF16)
    z_dr = dscr("z_dr", (32, 128, NTOK), BF16)
    yp_dr = dscr("yp_dr", (NCH, 128, INTER), BF16)
    kv_bi, kv_bo = dscr("kv_bi", (128, 2048)), dscr("kv_bo", (128, 2048))
    hal_bi, hal_bo = dscr("hal_bi", (128, 34 * 3)), dscr("hal_bo", (4 * 128, 34 * 3))
    ssd_bi, ssd_bo = dscr("ssd_bi", (128, INTER + NH)), dscr("ssd_bo", (4 * 128, INTER + NH))

    SB = {}
    def sball(name, part, cols, dt=F32):
        SB[name] = stack.enter_context(nc.sbuf_tensor(name, [part, cols], dt))
    sball("bfA", 128, 16384, BF16)
    sball("bfB", 128, 16384, BF16)
    sball("bfC", 128, 16384, BF16)
    sball("bigA", 128, 12288, F32)
    sball("sg", 128, 4096, F32)
    sball("pool", 128, 3072, F32)
    sball("wst", 128, 8192, BF16)
    sball("gsel", NH, 32 * 128, BF16)
    sball("eye", 64, 64, F32)
    sball("triu", 128, 128, BF16)
    sball("small", 128, 832, F32)
    sball("smb", 128, 2048, BF16)
    sball("sm64", 64, 2048, BF16)

    PS = [stack.enter_context(nc.psum_tensor(f"ps{i}", [128, 512], F32))
          for i in range(8)]

    CSB = SB["small"]
    def cc_(c0, c1, p0=0, p1=128):
        return CSB[p0:p1, c0:c1]

    def dma(eng, out_ap, in_ap, w=(), r=()):
        return P.op(eng, lambda h, o=out_ap, i=in_ap: h.dma_start(out=o, in_=i),
                    reads=r, writes=w, is_dma=True)

    def dmas(eng, out_ap, in_ap, w=(), r=()):
        return P.op(eng, lambda h, o=out_ap, i=in_ap: h.dma_start(
                        out=o, in_=i, allow_slow_non_contiguous=True),
                    reads=r, writes=w, is_dma=True)

    def dmat(out_ap, in_ap, w=(), r=()):
        return P.op("sync", lambda h, o=out_ap, i=in_ap: h.dma_start_transpose(out=o, in_=i),
                    reads=r, writes=w, is_dma=True)

    def vop(fn, r=(), w=()):
        return P.op("vector", fn, reads=r, writes=w)

    def aop(fn, r=(), w=()):
        return P.op("scalar", fn, reads=r, writes=w)

    def top(fn, r=(), w=()):
        return P.op("tensor", fn, reads=r, writes=w)

    ZCOL = cc_(261, 262)
    J2COL = cc_(263, 264)

    vop(lambda h: h.memset(cc_(261, 262), 0.0), w=("c_zero",))
    vop(lambda h: h.memset(cc_(262, 263), 1.0), w=("c_one",))
    vop(lambda h: h.memset(cc_(264, 265), 0.6), w=("c_06",))
    vop(lambda h: h.memset(cc_(265, 266), -0.33), w=("c_b1",))
    vop(lambda h: h.memset(cc_(266, 267), 0.33), w=("c_b2",))
    vop(lambda h: h.memset(cc_(267, 268), -0.2), w=("c_b3",))
    vop(lambda h: h.memset(cc_(268, 269), 0.2), w=("c_b4",))
    vop(lambda h: h.tensor_scalar(out=J2COL, in0=cc_(264, 265), scalar1=M52,
                                  scalar2=-M52, op0=ALU.add, op1=ALU.add),
        r=("c_06",), w=("c_J2",))

    dma("sync", SB["eye"][:], C_eye[0:64, 0:64], w=("eye",))
    dma("gpsimd", SB["triu"][:], C_triu, w=("triu",))
    dma("sync", SB["gsel"][:], C_G, w=("gsel",))
    dmas("sync", CSB[:, 96:160].rearrange("p (b k) -> p b k", k=16),
        b_in.rearrange("b (k p) -> p b k", p=128), w=("biases",))
    dmas("sync", CSB[:, 160:192].rearrange("p (b k) -> p b k", k=16),
        lnv.rearrange("b (k p) -> p b k", p=128), w=("lnpar",))
    dmas("sync", CSB[:, 192:224], nw_in.rearrange("(k p) -> p k", p=128), w=("nw",))
    dmas("sync", CSB[:, 304:440].rearrange("p (k f) -> p k f", f=4),
        conv_w.rearrange("(k p) f -> p k f", p=128), w=("convw",))
    dmas("sync", CSB[:, 224:258], conv_b.rearrange("(k p) -> p k", p=128), w=("convb",))
    dmas("sync", CSB[0:NH, 258:259], dtb_in.unsqueeze(1), w=("dtbias",))
    dmas("sync", CSB[0:NH, 259:260], A_in.unsqueeze(1), w=("Acol",))
    dmas("sync", CSB[0:NH, 260:261], D_in.unsqueeze(1), w=("Dcol",))
    dma("sync", CSB[:, 544:548], hsel, w=("hsel",))
    dma("sync", CSB[:, 548:552], ssel, w=("ssel",))

    vop(lambda h: h.tensor_copy(SB["smb"][0:NH, 0:1], CSB[0:NH, 260:261]),
        r=("Dcol",), w=("smb0",))
    for j in range(32):
        top(lambda h, j=j: h.matmul(PS[0][:, j:j + 1],
                                    SB["gsel"][0:NH, j * 128:(j + 1) * 128],
                                    SB["smb"][0:NH, 0:1], start=(j == 0), stop=(j == 31)),
            r=("gsel", "smb0", "ps0") if j == 0 else (("gsel", "smb0") if j == 31 else ()),
            w=("ps0",) if j in (0, 31) else ())
    vop(lambda h: h.tensor_copy(cc_(440, 472), PS[0][:, 0:32]), r=("ps0",), w=("dcolb",))

    SG0 = SB["sg"][:, 0:2048]
    SG1 = SB["sg"][:, 2048:4096]
    SMB0 = SB["smb"][:, 0:2048]

    TSTG = SB["wst"][:, 4096:6144]

    def tm2fm(srcname, src2d, dstbuf_ap, i, wkey, rkeys, nch=16, dst_arena=()):
        dmat(TSTG[:, 0:nch * 128].rearrange("p (k f) -> p k f", f=128),
             src2d.rearrange("p (k f) -> p k f", f=128),
             w=("wst45",), r=rkeys)
        dst = dstbuf_ap.rearrange("p (k f) -> p k f", f=NTOK)[:, 0:nch,
                                                              i * 128:(i + 1) * 128]
        dma("sync", dst, TSTG[:, 0:nch * 128].rearrange("p (k f) -> p k f", f=128),
            w=(wkey,) + tuple(dst_arena), r=("wst45",))

    def fm2tm(src_chunk2d, dstbuf_ap, k, wkey, rkeys, ntile=8, dstw=2048,
              dst_arena=()):
        dmat(TSTG[:, 0:ntile * 128].rearrange("p (i f) -> p i f", f=128),
             src_chunk2d.rearrange("p (i f) -> p i f", f=128),
             w=("wst45",), r=rkeys)
        dst = dstbuf_ap.rearrange("p (i f) -> p i f", f=dstw)[:, 0:ntile,
                                                              k * 128:(k + 1) * 128]
        dma("sync", dst, TSTG[:, 0:ntile * 128].rearrange("p (i f) -> p i f", f=128),
            w=(wkey,) + tuple(dst_arena), r=("wst45",))

    def ln_tile(i, src_ap, src_keys, tag, xnfm_buf, aq=False, tern=False,
                xn_arena=(), tq_arena=()):
        t = f"{tag}_{i}"
        dma("scalar", SG0, src_ap, w=("sg0",), r=src_keys)
        vop(lambda h: h.reduce_sum(out=cc_(0 + i, 1 + i), in_=SG0, axis=AX.X),
            r=("sg0",), w=(f"sum_{t}",))
        aop(lambda h: h.activation(out=SG1, in_=SG0, func=AF.Square,
                                   accum_out=cc_(8 + i, 9 + i)),
            r=("sg0",), w=(f"sq_{t}", "sg1"))
        vop(lambda h: h.tensor_scalar(out=cc_(16 + i, 17 + i), in0=cc_(0 + i, 1 + i),
                                      scalar1=1.0 / D, scalar2=None, op0=ALU.mult),
            r=(f"sum_{t}",), w=(f"mean_{t}",))
        vop(lambda h: h.tensor_scalar(out=cc_(24 + i, 25 + i), in0=cc_(8 + i, 9 + i),
                                      scalar1=1.0 / D, scalar2=None, op0=ALU.mult),
            r=(f"sq_{t}",), w=(f"ex2_{t}",))
        vop(lambda h: h.tensor_tensor(out=cc_(32 + i, 33 + i), in0=cc_(16 + i, 17 + i),
                                      in1=cc_(16 + i, 17 + i), op=ALU.mult),
            r=(f"mean_{t}",), w=(f"m2_{t}",))
        vop(lambda h: h.scalar_tensor_tensor(out=cc_(32 + i, 33 + i),
                                             in0=cc_(24 + i, 25 + i), scalar=LN_EPS,
                                             in1=cc_(32 + i, 33 + i),
                                             op0=ALU.add, op1=ALU.subtract),
            r=(f"ex2_{t}", f"m2_{t}"), w=(f"vare_{t}",))
        aop(lambda h: h.activation(out=cc_(32 + i, 33 + i), in_=cc_(32 + i, 33 + i),
                                   func=AF.Sqrt),
            r=(f"vare_{t}",), w=(f"stdd_{t}",))
        vop(lambda h: h.reciprocal(out=cc_(40 + i, 41 + i), in_=cc_(32 + i, 33 + i)),
            r=(f"stdd_{t}",), w=(f"rstd_{t}",))
        vop(lambda h: h.scalar_tensor_tensor(out=cc_(48 + i, 49 + i),
                                             in0=cc_(16 + i, 17 + i), scalar=-1.0,
                                             in1=cc_(40 + i, 41 + i),
                                             op0=ALU.mult, op1=ALU.mult),
            r=(f"mean_{t}", f"rstd_{t}"), w=(f"nmr_{t}",))
        aop(lambda h: h.activation(out=SG1, in_=SG0, func=AF.Identity,
                                   bias=cc_(48 + i, 49 + i), scale=cc_(40 + i, 41 + i)),
            r=("sg0", f"rstd_{t}", f"nmr_{t}", "sg1"), w=("sg1",))
        vop(lambda h: h.tensor_copy(SMB0, SG1), r=("sg1",), w=("smb0",))
        tm2fm("smb", SMB0, SB[xnfm_buf][:], i, f"xnfm_{t}", ("smb0",),
              dst_arena=xn_arena)
        if tern:
            aop(lambda h: h.activation(out=SG0, in_=SG1, func=AF.Sign,
                                       bias=cc_(265, 266)),
                r=("sg1", "sg0", "c_b1"), w=("sg0",))
            aop(lambda h: h.activation(out=SMB0, in_=SG1, func=AF.Sign,
                                       bias=cc_(266, 267)),
                r=("sg1", "smb0", "c_b2"), w=("smb0",))
            vop(lambda h: h.tensor_tensor(out=SMB0, in0=SMB0, in1=SG0, op=ALU.add),
                r=("sg0", "smb0"), w=("smb0",))
            tm2fm("smb", SMB0, SB["bigA"][:].bitcast(BF16)[:, 0:KC * NTOK], i,
                  f"tqfm_{t}", ("smb0",), dst_arena=tq_arena)
        if aq:
            vop(lambda h: h.reduce_sum(out=cc_(56 + i, 57 + i), in_=SG1, axis=AX.X),
                r=("sg1",), w=(f"rsum_{t}",))
            aop(lambda h: h.activation(out=SG0, in_=SG1, func=AF.Square,
                                       accum_out=cc_(64 + i, 65 + i)),
                r=("sg1", "sg0"), w=(f"rsq_{t}", "sg0"))
            vop(lambda h: h.tensor_scalar(out=cc_(56 + i, 57 + i), in0=cc_(56 + i, 57 + i),
                                          scalar1=1.0 / D, scalar2=None, op0=ALU.mult),
                r=(f"rsum_{t}",), w=(f"rmean_{t}",))
            vop(lambda h: h.tensor_scalar(out=cc_(64 + i, 65 + i), in0=cc_(64 + i, 65 + i),
                                          scalar1=1.0 / D, scalar2=None, op0=ALU.mult),
                r=(f"rsq_{t}",), w=(f"rex2_{t}",))
            vop(lambda h: h.tensor_tensor(out=cc_(72 + i, 73 + i), in0=cc_(56 + i, 57 + i),
                                          in1=cc_(56 + i, 57 + i), op=ALU.mult),
                r=(f"rmean_{t}",), w=(f"rm2_{t}",))
            vop(lambda h: h.scalar_tensor_tensor(out=cc_(72 + i, 73 + i),
                                                 in0=cc_(64 + i, 65 + i), scalar=RMS_EPS,
                                                 in1=cc_(72 + i, 73 + i),
                                                 op0=ALU.add, op1=ALU.subtract),
                r=(f"rex2_{t}", f"rm2_{t}"), w=(f"rvar_{t}",))
            aop(lambda h: h.activation(out=cc_(72 + i, 73 + i), in_=cc_(72 + i, 73 + i),
                                       func=AF.Sqrt),
                r=(f"rvar_{t}",), w=(f"rstd2_{t}",))
            vop(lambda h: h.reciprocal(out=cc_(80 + i, 81 + i), in_=cc_(72 + i, 73 + i)),
                r=(f"rstd2_{t}",), w=(f"rr_{t}",))
            vop(lambda h: h.scalar_tensor_tensor(out=cc_(72 + i, 73 + i),
                                                 in0=cc_(56 + i, 57 + i), scalar=-1.0,
                                                 in1=cc_(80 + i, 81 + i),
                                                 op0=ALU.mult, op1=ALU.mult),
                r=(f"rmean_{t}", f"rr_{t}"), w=(f"rnmr_{t}",))
            aop(lambda h: h.activation(out=SG0, in_=SG1, func=AF.Identity,
                                       bias=cc_(72 + i, 73 + i), scale=cc_(80 + i, 81 + i)),
                r=("sg1", f"rr_{t}", f"rnmr_{t}", "sg0"), w=("sg0",))
            vop(lambda h: h.reduce_max(out=cc_(56 + i, 57 + i), in_=SG0, axis=AX.X,
                                       apply_absolute_value=True),
                r=("sg0",), w=(f"amax_{t}",))
            vop(lambda h: h.reciprocal(out=cc_(64 + i, 65 + i), in_=cc_(56 + i, 57 + i)),
                r=(f"amax_{t}",), w=(f"ram_{t}",))
            vop(lambda h: h.tensor_scalar(out=cc_(56 + i, 57 + i), in0=cc_(64 + i, 65 + i),
                                          scalar1=127.0, scalar2=None, op0=ALU.mult),
                r=(f"ram_{t}",), w=(f"sS_{t}",))
            vop(lambda h: h.reciprocal(out=cc_(64 + i, 65 + i), in_=cc_(56 + i, 57 + i)),
                r=(f"sS_{t}",), w=(f"invs_{t}",))
            vop(lambda h: h.tensor_scalar(out=SG1, in0=SG0, scalar1=cc_(56 + i, 57 + i),
                                          scalar2=None, op0=ALU.mult),
                r=("sg0", f"sS_{t}", "sg1"), w=("sg1",))
            vop(lambda h: h.tensor_scalar(out=SG0, in0=SG1, scalar1=M32, scalar2=-M32,
                                          op0=ALU.add, op1=ALU.add),
                r=("sg1", "sg0"), w=("sg0",))
            vop(lambda h: h.tensor_scalar(out=SG1, in0=SG1, scalar1=M52, scalar2=-M52,
                                          op0=ALU.add, op1=ALU.add),
                r=("sg1",), w=("sg1",))
            vop(lambda h: h.tensor_tensor(out=SG1, in0=SG1, in1=SG0, op=ALU.subtract),
                r=("sg1", "sg0"), w=("sg1",))
            vop(lambda h: h.scalar_tensor_tensor(out=SG1, in0=SG1, scalar=J2COL,
                                                 in1=SG0, op0=ALU.mult, op1=ALU.add),
                r=("sg1", "sg0", "c_J2"), w=("sg1",))
            vop(lambda h: h.tensor_scalar(out=SMB0, in0=SG1,
                                          scalar1=cc_(64 + i, 65 + i), scalar2=None,
                                          op0=ALU.mult),
                r=("sg1", f"invs_{t}", "smb0"), w=("smb0",))
            tm2fm("smb", SMB0, SB["bfB"][:], i, f"xqfm_{t}", ("smb0",))

    for i in range(8):
        src = x_in.rearrange("(i p) d -> p i d", p=128)[:, i, :]
        ln_tile(i, src, (), "l0", "bfA", aq=True)

    XN0 = tuple(f"xnfm_l0_{i}" for i in range(8))
    XQ0 = tuple(f"xqfm_l0_{i}" for i in range(8))

    NWSL = 8
    WSL = [SB["wst"][:, j * 512:(j + 1) * 512] for j in range(NWSL)]
    wslotc = [0]

    def wtile():
        s = wslotc[0] % NWSL
        wslotc[0] += 1
        return WSL[s], f"wt{s}", "gpsimd"

    def fm_proj(Wap, nout_chunks, rhs_buf, rhs_keys, evac, tag, psbase=0,
                kcn=KC, rhs_cols=NTOK):
        for cg0 in range(0, nout_chunks, 4):
            cgn = min(4, nout_chunks - cg0)
            for k in range(kcn):
                wb, wkey, q = wtile()
                dma(q, wb[:, 0:cgn * 128],
                    Wap[k * 128:(k + 1) * 128, cg0 * 128:cg0 * 128 + cgn * 128],
                    w=(wkey,))
                for c in range(cgn):
                    for nt in range(2):
                        pk = f"ps{2 * c + nt}"
                        rk = (wkey,) + (rhs_keys if k in (0, kcn - 1) else ())
                        top(lambda h, k=k, c=c, nt=nt, wb=wb:
                            h.matmul(PS[2 * c + nt][:, 0:512],
                                     wb[:, c * 128:(c + 1) * 128],
                                     SB[rhs_buf][:, k * rhs_cols + nt * 512:
                                                 k * rhs_cols + nt * 512 + 512],
                                     start=(k == 0), stop=(k == kcn - 1)),
                            r=rk if k in (0, kcn - 1) else (wkey,),
                            w=(pk,) if k in (0, kcn - 1) else ())
            for c in range(cgn):
                for nt in range(2):
                    evac(cg0 + c, nt * 512, 512, PS[2 * c + nt], f"ps{2 * c + nt}")

    def evac_qk(c, nt0, ntw, ps, pskey):
        pp = (2 * c + nt0 // 512) % 4
        a_ = SG0[:, pp * 512:pp * 512 + ntw]
        b_ = SG1[:, pp * 512:pp * 512 + ntw]
        ka, kb = f"sgp{pp}", f"sg1p{pp}"
        aop(lambda h, a_=a_: h.activation(out=a_, in_=ps[:, 0:ntw],
                                          func=AF.Identity, bias=CSB[:, 96 + c:97 + c]),
            r=(pskey, "biases", ka), w=(ka,))
        vop(lambda h, a_=a_, b_=b_: h.tensor_scalar(out=b_, in0=a_,
                                                    scalar1=0.0, scalar2=None,
                                                    op0=ALU.min),
            r=(ka, kb), w=(kb,))
        aop(lambda h, b_=b_: h.activation(out=b_, in_=b_, func=AF.Exp),
            r=(kb,), w=(kb,))
        o = SB["bfC"][:, c * NTOK + nt0:c * NTOK + nt0 + ntw]
        vop(lambda h, o=o, a_=a_, b_=b_: h.scalar_tensor_tensor(
                out=o, in0=b_, scalar=-1.0, in1=a_, op0=ALU.add, op1=ALU.max),
            r=(kb, ka), w=(f"qke_{c}_{nt0}",))

    P.retag(("sg0",), ("sgp0", "sgp1", "sgp2", "sgp3"))
    P.retag(("sg1",), ("sg1p0", "sg1p1", "sg1p2", "sg1p3"))
    fm_proj(Wqk, KC, "bfA", XN0, evac_qk, "qk", psbase=0)
    P.retag(("sgp0", "sgp1", "sgp2", "sgp3"), ("sg0",))
    P.retag(("sg1p0", "sg1p1", "sg1p2", "sg1p3"), ("sg1",))
    QKE = tuple(f"qke_{c}_{nt}" for c in range(KC) for nt in (0, 512))

    def evac_u(c, nt0, ntw, ps, pskey):
        o = SB["smb"][:, nt0:nt0 + ntw]
        aop(lambda h, o=o: h.activation(out=o, in_=ps[:, 0:ntw], func=AF.Silu,
                                        bias=CSB[:, 128 + c:129 + c]),
            r=(pskey, "biases", "smb0"), w=("smb0",))
        dma("scalar", u_dr[c, :, nt0:nt0 + ntw], o, w=(f"udr_{c}_{nt0}",), r=("smb0",))

    fm_proj(Wu, KC, "bfA", XN0, evac_u, "u", psbase=4)

    VTM = SB["wst"][:, 4096:5120]
    QTM = SB["wst"][:, 5120:6144]

    def v_head_tail(hh):
        dmat(VTM.rearrange("p (i f) -> p i f", f=128),
             SB["smb"][:, 0:1024].rearrange("p (i f) -> p i f", f=128),
             w=("wst45",), r=("smb0",))
        dmat(QTM.rearrange("p (i f) -> p i f", f=128),
             SB["bfC"][:, hh * NTOK:(hh + 1) * NTOK].rearrange("p (i f) -> p i f", f=128),
             w=("wst45",), r=(f"qke_{hh}_0", f"qke_{hh}_512"))
        for i in range(8):
            top(lambda h, i=i: h.matmul(PS[7][:, 0:128],
                                        QTM[:, i * 128:(i + 1) * 128],
                                        VTM[:, i * 128:(i + 1) * 128],
                                        start=(i == 0), stop=(i == 7)),
                r=("wst45", "ps7") if i == 0 else (("wst45",) if i == 7 else ()),
                w=("ps7",) if i in (0, 7) else ())
        vop(lambda h, hh=hh: h.tensor_copy(SB["pool"][:, hh * 128:(hh + 1) * 128],
                                           PS[7][:, 0:128]),
            r=("ps7",), w=(f"kv_{hh}",))

    for cg0 in range(0, KC, 3):
        cgn = min(3, KC - cg0)
        for k in range(KC):
            wb, wkey, q = wtile()
            dma(q, wb[:, 0:cgn * 128],
                Wv[k * 128:(k + 1) * 128, cg0 * 128:cg0 * 128 + cgn * 128], w=(wkey,))
            for c in range(cgn):
                for nt in range(2):
                    ps = PS[2 * c + nt]
                    pk = f"ps{2 * c + nt}"
                    rk = (wkey,) + (XN0 if k in (0, KC - 1) else ())
                    top(lambda h, k=k, c=c, nt=nt, ps=ps, wb=wb:
                        h.matmul(ps[:, 0:512], wb[:, c * 128:(c + 1) * 128],
                                 SB["bfA"][:, k * NTOK + nt * 512:k * NTOK + (nt + 1) * 512],
                                 start=(k == 0), stop=(k == KC - 1)),
                        r=rk if k in (0, KC - 1) else (wkey,),
                        w=(pk,) if k in (0, KC - 1) else ())
        for c in range(cgn):
            hh = cg0 + c
            for nt in range(2):
                o = SB["smb"][:, nt * 512:(nt + 1) * 512]
                aop(lambda h, o=o, c=c, nt=nt, hh=hh: h.activation(
                        out=o, in_=PS[2 * c + nt][:, 0:512], func=AF.Silu,
                        bias=CSB[:, 112 + hh:113 + hh]),
                    r=(f"ps{2 * c + nt}", "biases", "smb0"), w=("smb0",))
            v_head_tail(hh)

    dma("gpsimd", kv_bi, SB["pool"][:, 0:2048], w=("kvbi",),
        r=tuple(f"kv_{hh}" for hh in range(16)))
    if SKIP_CC:
        dma("gpsimd", kv_bo, kv_bi, w=("kvbo",), r=("kvbi",))
    else:
        P.op("gpsimd", lambda h: h.collective_compute(
                "AllReduce", ALU.add, replica_groups=[[0, 1, 2, 3], [4, 5, 6, 7]],
                ins=[kv_bi.opt()], outs=[kv_bo.opt()]),
            reads=("kvbi",), writes=("kvbo",), is_cc=True)
    dma("gpsimd", SB["pool"][:, 0:2048], kv_bo, w=("kvr",), r=("kvbo",) + tuple(
        f"kv_{hh}" for hh in range(16)))

    P.retag(("sg0",), ("sgp0", "sgp1", "sgp2", "sgp3"))
    P.retag(tuple(f"wt{j}" for j in range(NWSL)) + ("wst45",),
            ("wstA", "wstB"))
    PKB = SB["bigA"][:].bitcast(BF16)
    WSLOT = [(SB["wst"][:, 0:2048], SB["wst"][:, 2048:4096], "wstA", "gpsimd"),
             (SB["wst"][:, 4096:6144], SB["wst"][:, 6144:8192], "wstB", "gpsimd"),
             (PKB[:, 0:2048], PKB[:, 2048:4096], "wstC", "gpsimd"),
             (PKB[:, 4096:6144], PKB[:, 6144:8192], "wstD", "gpsimd")]

    def sign_T(Wap, row0, slot, ncols=2048):
        stg, dst, key, q = WSLOT[slot]
        dma(q, stg[:, 0:ncols], Wap[row0:row0 + 128, 0:ncols], w=(key,))
        aop(lambda h, stg=stg, ncols=ncols: h.activation(
                out=stg[:, 0:ncols], in_=stg[:, 0:ncols], func=AF.Sign),
            r=(key,), w=(key,))
        dmat(dst[:, 0:ncols].rearrange("p (k f) -> p k f", f=128),
             stg[:, 0:ncols].rearrange("p (k f) -> p k f", f=128),
             w=(key,), r=(key,))
        return dst, key

    for c in range(HF // 128):
        tg, kg = sign_T(Wg, c * 128, (2 * c) % 4)
        tu, ku = sign_T(Wu2, c * 128, (2 * c + 1) % 4)
        for k in range(KC):
            for nt in range(2):
                top(lambda h, k=k, nt=nt, tg=tg: h.matmul(
                        PS[nt][:, 0:512], tg[:, k * 128:(k + 1) * 128],
                        SB["bfB"][:, k * NTOK + nt * 512:k * NTOK + (nt + 1) * 512],
                        start=(k == 0), stop=(k == KC - 1)),
                    r=(kg,) + (XQ0 if k in (0, KC - 1) else ()),
                    w=(f"ps{nt}",) if k in (0, KC - 1) else ())
        for k in range(KC):
            for nt in range(2):
                top(lambda h, k=k, nt=nt, tu=tu: h.matmul(
                        PS[2 + nt][:, 0:512], tu[:, k * 128:(k + 1) * 128],
                        SB["bfB"][:, k * NTOK + nt * 512:k * NTOK + (nt + 1) * 512],
                        start=(k == 0), stop=(k == KC - 1)),
                    r=(ku,) + (XQ0 if k in (0, KC - 1) else ()),
                    w=(f"ps{2 + nt}",) if k in (0, KC - 1) else ())
        for nt in range(2):
            pp = (2 * c + nt) % 4
            aop(lambda h, nt=nt, pp=pp: h.activation(
                    out=SG0[:, pp * 512:(pp + 1) * 512],
                    in_=PS[nt][:, 0:512], func=AF.Silu),
                r=(f"ps{nt}", f"sgp{pp}"), w=(f"sgp{pp}",))
            vop(lambda h, nt=nt, pp=pp: h.tensor_tensor(
                    out=SB["smb"][:, nt * 512:(nt + 1) * 512],
                    in0=SG0[:, pp * 512:(pp + 1) * 512],
                    in1=PS[2 + nt][:, 0:512], op=ALU.mult),
                r=(f"sgp{pp}", f"ps{2 + nt}", "smb0"), w=("smb0",))
        dma("scalar", p_dr[c, :, :], SB["smb"][:, 0:1024], w=(f"pdr_{c}",), r=("smb0",))
        if c % 4 == 3:
            oc = c // 4
            dma("gpsimd", SB["bfA"][:, 0:8192], Wd[oc * 128:(oc + 1) * 128, :],
                w=("bfA_wd",))
            aop(lambda h: h.activation(out=SB["bfA"][:, 0:8192],
                                       in_=SB["bfA"][:, 0:8192], func=AF.Sign),
                r=("bfA_wd",), w=("bfA_wd",))
            dmat(SB["bfA"][:, 8192:16384].rearrange("p (k f) -> p k f", f=128),
                 SB["bfA"][:, 0:8192].rearrange("p (k f) -> p k f", f=128),
                 w=("bfA_wdT",), r=("bfA_wd",))
            dma("scalar", wdT_dr[oc, :, :], SB["bfA"][:, 8192:16384], w=(f"wdT_{oc}",),
                r=("bfA_wdT",))
    P.retag(("sgp0", "sgp1", "sgp2", "sgp3"), ("sg0",))
    P.retag(("wstA", "wstB"),
            tuple(f"wt{j}" for j in range(NWSL)) + ("wst45",))

    P.retag(("wstC", "wstD"), ("pkh0a", "pkh0b", "pkh1a", "pkh1b"))
    P.retag(("bfA_wd", "bfA_wdT"), ("wdsl0", "wdsl1"))
    WDSL = [SB["bfA"][:, 0:2048], SB["bfA"][:, 2048:4096]]
    for og in range(2):
        for tt in range(2):
            for kq in range(4):
                qs = kq % 2
                pka, pkb2 = f"pkh{qs}a", f"pkh{qs}b"
                prr = p_dr.rearrange("c p t -> p c t")
                dma("gpsimd", PKB[:, qs * 8192:qs * 8192 + 4096],
                    prr[:, kq * 16:kq * 16 + 8, tt * 512:(tt + 1) * 512],
                    w=(pka,),
                    r=tuple(f"pdr_{c2}" for c2 in range(kq * 16, kq * 16 + 8)))
                dma("scalar", PKB[:, qs * 8192 + 4096:(qs + 1) * 8192],
                    prr[:, kq * 16 + 8:(kq + 1) * 16, tt * 512:(tt + 1) * 512],
                    w=(pkb2,),
                    r=tuple(f"pdr_{c2}" for c2 in range(kq * 16 + 8, (kq + 1) * 16)))
                for c in range(8):
                    oc = og * 8 + c
                    sl = (oc + kq) % 2
                    wk = f"wdsl{sl}"
                    dma("sync", WDSL[sl], wdT_dr[oc, :, kq * 2048:(kq + 1) * 2048],
                        w=(wk,), r=(f"wdT_{oc}",))
                    for kk in range(16):
                        pr = (pka,) if kk in (0, 7) else ((pkb2,) if kk in (8, 15) else ())
                        top(lambda h, kk=kk, c=c, sl=sl, kq=kq, qs=qs: h.matmul(
                                PS[c][:, 0:512],
                                WDSL[sl][:, kk * 128:(kk + 1) * 128],
                                PKB[:, qs * 8192 + kk * 512:qs * 8192 + (kk + 1) * 512],
                                start=(kq == 0 and kk == 0), stop=(kq == 3 and kk == 15)),
                            r=(wk,) + pr,
                            w=(f"ps{c}",) if (kq, kk) in ((0, 0), (3, 15)) else ())
            for c in range(8):
                oc = og * 8 + c
                aop(lambda h, c=c: h.activation(out=SB["smb"][:, 0:512],
                                                in_=PS[c][:, 0:512], func=AF.Silu),
                    r=(f"ps{c}", "smb0"), w=("smb0",))
                dma("scalar", g_dr[oc, :, tt * 512:(tt + 1) * 512], SB["smb"][:, 0:512],
                    w=(f"gdr_{oc}_{tt}",), r=("smb0",))

    GDR = tuple(f"gdr_{oc}_{tt}" for oc in range(16) for tt in range(2))

    OUTB = PKB
    P.retag(("pkh0a", "pkh0b", "pkh1a", "pkh1b"), ("outb",))
    P.retag(XQ0, ("tb",))
    TBB = SB["bfB"][:]

    for hh in range(16):
        vop(lambda h, hh=hh: h.tensor_copy(SB["smb"][:, 0:128],
                                           SB["pool"][:, hh * 128:(hh + 1) * 128]),
            r=("kvr",), w=("smb0",))
        vop(lambda h, hh=hh: h.scalar_tensor_tensor(
                out=SB["smb"][:, 128:256], in0=SB["smb"][:, 0:128], scalar=-1.0,
                in1=SB["pool"][:, hh * 128:(hh + 1) * 128],
                op0=ALU.mult, op1=ALU.add),
            r=("kvr", "smb0"), w=("smb0",))
        for nt in range(2):
            top(lambda h, hh=hh, nt=nt: h.matmul(
                    PS[nt][:, 0:512], SB["smb"][:, 0:128],
                    SB["bfC"][:, hh * NTOK + nt * 512:hh * NTOK + (nt + 1) * 512],
                    start=True, stop=False),
                r=("smb0", f"qke_{hh}_{nt * 512}", f"ps{nt}"), w=(f"ps{nt}",))
            top(lambda h, hh=hh, nt=nt: h.matmul(
                    PS[nt][:, 0:512], SB["smb"][:, 128:256],
                    SB["bfC"][:, hh * NTOK + nt * 512:hh * NTOK + (nt + 1) * 512],
                    start=False, stop=True),
                r=("smb0", f"qke_{hh}_{nt * 512}"), w=(f"ps{nt}",))
            vop(lambda h, hh=hh, nt=nt: h.tensor_copy(
                    OUTB[:, hh * NTOK + nt * 512:hh * NTOK + (nt + 1) * 512],
                    PS[nt][:, 0:512]),
                r=(f"ps{nt}", "outb"), w=(f"outb_{hh}_{nt}",))

    OUTK = tuple(f"outb_{hh}_{nt}" for hh in range(16) for nt in range(2))

    for hh in range(16):
        aop(lambda h, hh=hh: h.activation(
                out=TBB[:, hh * NTOK:(hh + 1) * NTOK],
                in_=OUTB[:, hh * NTOK:(hh + 1) * NTOK], func=AF.Square),
            r=(f"outb_{hh}_0", f"outb_{hh}_1", "tb"), w=(f"sqb_{hh}",))
    for nt in range(2):
        for hh in range(16):
            top(lambda h, hh=hh, nt=nt: h.matmul(
                    PS[4 + nt][0:1, 0:512], SB["triu"][:, 127:128],
                    OUTB[:, hh * NTOK + nt * 512:hh * NTOK + (nt + 1) * 512],
                    start=(hh == 0), stop=(hh == 15)),
                r=("triu", f"outb_{hh}_{nt}", f"ps{4 + nt}") if hh == 0
                  else ("triu", f"outb_{hh}_{nt}"),
                w=(f"ps{4 + nt}",) if hh in (0, 15) else ())
            top(lambda h, hh=hh, nt=nt: h.matmul(
                    PS[6 + nt][0:1, 0:512], SB["triu"][:, 127:128],
                    TBB[:, hh * NTOK + nt * 512:hh * NTOK + (nt + 1) * 512],
                    start=(hh == 0), stop=(hh == 15)),
                r=("triu", f"sqb_{hh}", f"ps{6 + nt}") if hh == 0
                  else ("triu", f"sqb_{hh}"),
                w=(f"ps{6 + nt}",) if hh in (0, 15) else ())
    P.retag(tuple(f"kv_{hh}" for hh in range(16)) + ("kvr",), ("poolb",))
    RSUM = SB["sg"][0:1, 0:1024]
    RSQ = SB["sg"][0:1, 1024:2048]
    for nt in range(2):
        vop(lambda h, nt=nt: h.tensor_copy(SB["sg"][0:1, nt * 512:(nt + 1) * 512],
                                           PS[4 + nt][0:1, 0:512]),
            r=(f"ps{4 + nt}", "sg0"), w=("sg0",))
        vop(lambda h, nt=nt: h.tensor_copy(SB["sg"][0:1, 1024 + nt * 512:1024 + (nt + 1) * 512],
                                           PS[6 + nt][0:1, 0:512]),
            r=(f"ps{6 + nt}", "sg0"), w=("sg0",))
    MEANR = SB["sg"][0:1, 2048:3072]
    NMRR = SB["sg"][0:1, 3072:4096]
    vop(lambda h: h.tensor_scalar(out=MEANR, in0=RSUM, scalar1=1.0 / D, scalar2=None,
                                  op0=ALU.mult), r=("sg0", "sg1"), w=("sg1",))
    vop(lambda h: h.tensor_scalar(out=RSQ, in0=RSQ, scalar1=1.0 / D, scalar2=None,
                                  op0=ALU.mult), r=("sg0",), w=("sg0",))
    vop(lambda h: h.tensor_tensor(out=NMRR, in0=MEANR, in1=MEANR, op=ALU.mult),
        r=("sg1",), w=("sg1",))
    vop(lambda h: h.scalar_tensor_tensor(out=RSQ, in0=RSQ, scalar=LN_EPS, in1=NMRR,
                                         op0=ALU.add, op1=ALU.subtract),
        r=("sg0", "sg1"), w=("sg0",))
    aop(lambda h: h.activation(out=RSQ, in_=RSQ, func=AF.Sqrt), r=("sg0",), w=("sg0",))
    vop(lambda h: h.reciprocal(out=RSUM, in_=RSQ), r=("sg0",), w=("sg0",))
    vop(lambda h: h.scalar_tensor_tensor(out=NMRR, in0=MEANR, scalar=-1.0, in1=RSUM,
                                         op0=ALU.mult, op1=ALU.mult),
        r=("sg0", "sg1"), w=("sg1",))
    vop(lambda h: h.memset(CSB[0:1, 704:832], 1.0), w=("onesr",))
    for nt in range(2):
        top(lambda h, nt=nt: h.matmul(PS[4 + nt][:, 0:512], CSB[0:1, 704:832],
                                      SB["sg"][0:1, nt * 512:(nt + 1) * 512],
                                      start=True, stop=True),
            r=("onesr", "sg0", f"ps{4 + nt}"), w=(f"ps{4 + nt}",))
        top(lambda h, nt=nt: h.matmul(PS[6 + nt][:, 0:512], CSB[0:1, 704:832],
                                      SB["sg"][0:1, 3072 + nt * 512:3072 + (nt + 1) * 512],
                                      start=True, stop=True),
            r=("onesr", "sg1", f"ps{6 + nt}"), w=(f"ps{6 + nt}",))
        vop(lambda h, nt=nt: h.tensor_copy(SB["pool"][:, nt * 512:(nt + 1) * 512],
                                           PS[4 + nt][:, 0:512]),
            r=(f"ps{4 + nt}", "poolb"), w=("poolb",))
        vop(lambda h, nt=nt: h.tensor_copy(SB["pool"][:, 1024 + nt * 512:1024 + (nt + 1) * 512],
                                           PS[6 + nt][:, 0:512]),
            r=(f"ps{6 + nt}", "poolb"), w=("poolb",))
    RB = SB["pool"][:, 0:1024]
    BB = SB["pool"][:, 1024:2048]

    for hh in range(16):
        vop(lambda h, hh=hh: h.tensor_tensor(out=SG0[:, 0:1024],
                                             in0=OUTB[:, hh * NTOK:(hh + 1) * NTOK],
                                             in1=RB, op=ALU.mult),
            r=(f"outb_{hh}_0", f"outb_{hh}_1", "poolb", "sg0"), w=("sg0",))
        vop(lambda h: h.tensor_tensor(out=SG0[:, 0:1024], in0=SG0[:, 0:1024], in1=BB,
                                      op=ALU.add),
            r=("sg0", "poolb"), w=("sg0",))
        aop(lambda h, hh=hh: h.activation(out=SG0[:, 0:1024], in_=SG0[:, 0:1024],
                                          func=AF.Identity,
                                          bias=CSB[:, 176 + hh:177 + hh],
                                          scale=CSB[:, 160 + hh:161 + hh]),
            r=("sg0", "lnpar"), w=("sg0",))
        dma("scalar", SB["smb"][:, 0:1024], u_dr[hh, :, :], w=("smb0",),
            r=(f"udr_{hh}_0", f"udr_{hh}_512", "smb0"))
        vop(lambda h, hh=hh: h.tensor_tensor(
                out=TBB[:, hh * NTOK:(hh + 1) * NTOK],
                in0=SG0[:, 0:1024], in1=SB["smb"][:, 0:1024], op=ALU.mult),
            r=("sg0", "smb0"), w=(f"tfm_{hh}", f"sqb_{hh}"))
    TFM = tuple(f"tfm_{hh}" for hh in range(16))

    P.retag(QKE, ("bfC_s",))

    def rhs_t(k, nt0, ntw):
        return TBB[:, k * NTOK + nt0:k * NTOK + nt0 + ntw]

    def evac_s(c, nt0, ntw, ps, pskey):
        aop(lambda h, nt0=nt0, ntw=ntw: h.activation(
                out=SG1[:, nt0:nt0 + ntw], in_=ps[:, 0:ntw], func=AF.Identity,
                bias=CSB[:, 144 + c:145 + c]),
            r=(pskey, "biases", "sg1"), w=("sg1",))
        if nt0 == 512:
            hh = c
            dma("scalar", SB["smb"][:, 0:1024], g_dr[hh, :, :], w=("smb0",),
                r=(f"gdr_{hh}_0", f"gdr_{hh}_1", "smb0"))
            vop(lambda h: h.tensor_tensor(out=SG0[:, 0:1024], in0=SG1[:, 0:1024],
                                          in1=SB["smb"][:, 0:1024], op=ALU.mult),
                r=("sg1", "smb0", "sg0"), w=("sg0",))
            vop(lambda h, hh=hh: h.tensor_copy(OUTB[:, hh * NTOK:(hh + 1) * NTOK],
                                               SG0[:, 0:1024]),
                r=("sg0",), w=(f"hi_{hh}", f"outb_{hh}_0", f"outb_{hh}_1"))
            vop(lambda h, hh=hh: h.scalar_tensor_tensor(
                    out=SB["bfC"][:, hh * NTOK:(hh + 1) * NTOK],
                    in0=OUTB[:, hh * NTOK:(hh + 1) * NTOK], scalar=-1.0,
                    in1=SG0[:, 0:1024], op0=ALU.mult, op1=ALU.add),
                r=(f"hi_{hh}", "sg0", "bfC_s"), w=(f"lo_{hh}",))

    if DEBUG_L0:
        for k2 in range(16):
            vop(lambda h, k2=k2: h.tensor_copy(SG0[:, 0:1024],
                                               TBB[:, k2 * NTOK:(k2 + 1) * NTOK]),
                r=TFM + ("sg0",), w=("sg0",))
            dma("gpsimd", dbg_t[k2, :, :], SG0[:, 0:1024], w=(f"dbt_{k2}",), r=("sg0",))
            vop(lambda h, k2=k2: h.tensor_copy(SG0[:, 0:1024],
                                               OUTB[:, k2 * NTOK:(k2 + 1) * NTOK]),
                r=tuple(f"outb_{hh}_{nt} " .strip() for hh in (k2,) for nt in (0, 1)) + ("sg0",),
                w=("sg0",))
            dma("gpsimd", dbg_o[k2, :, :], SG0[:, 0:1024], w=(f"dbo_{k2}",), r=("sg0",))

    for cg0 in range(0, KC, 4):
        cgn = min(4, KC - cg0)
        for k in range(KC):
            wb, wkey, q = wtile()
            dma(q, wb[:, 0:cgn * 128],
                Wo[k * 128:(k + 1) * 128, cg0 * 128:cg0 * 128 + cgn * 128], w=(wkey,))
            for c in range(cgn):
                for nt in range(2):
                    ps = PS[2 * c + nt] if cgn <= 4 else None
                    pk = f"ps{2 * c + nt}"
                    rk = (wkey,) + (TFM if k in (0, KC - 1) else ())
                    top(lambda h, k=k, c=c, nt=nt, wb=wb:
                        h.matmul(PS[2 * c + nt][:, 0:512], wb[:, c * 128:(c + 1) * 128],
                                 rhs_t(k, nt * 512, 512),
                                 start=(k == 0), stop=(k == KC - 1)),
                        r=rk if k in (0, KC - 1) else (wkey,),
                        w=(pk,) if k in (0, KC - 1) else ())
        for c in range(cgn):
            for nt in range(2):
                evac_s(cg0 + c, nt * 512, 512, PS[2 * c + nt], f"ps{2 * c + nt}")
    SK = ()

    P.retag(("wdsl0", "wdsl1"), ("bfA_hi",))
    if DEBUG_L0:
        for k2 in range(16):
            dma("scalar", SB["smb"][:, 0:1024], g_dr[k2, :, :], w=("smb0",),
                r=(f"gdr_{k2}_0", f"gdr_{k2}_1", "smb0"))
            vop(lambda h: h.tensor_copy(SG0[:, 0:1024], SB["smb"][:, 0:1024]),
                r=("smb0", "sg0"), w=("sg0",))
            dma("gpsimd", dbg_g[k2, :, :], SG0[:, 0:1024], w=(f"dbg_{k2}",), r=("sg0",))

    P.retag(TFM + tuple(f"sqb_{k}" for k in range(16)), ("bfB_lotm",))
    for k in range(16):
        fm2tm(OUTB[:, k * NTOK:(k + 1) * NTOK], SB["bfA"][:], k, f"hitm_{k}",
              (f"hi_{k}",), dst_arena=("bfA_hi",))
        fm2tm(SB["bfC"][:, k * NTOK:(k + 1) * NTOK], SB["bfB"][:], k, f"lotm_{k}",
              (f"lo_{k}",), dst_arena=("bfB_lotm",))
    HITM = tuple(f"hitm_{k}" for k in range(16))
    LOTM = tuple(f"lotm_{k}" for k in range(16))

    if DEBUG_L0:
        for k2 in range(16):
            vop(lambda h, k2=k2: h.tensor_copy(SG0[:, 0:1024],
                                               OUTB[:, k2 * NTOK:(k2 + 1) * NTOK]),
                r=(f"hi_{k2}", "sg0"), w=("sg0",))
            dma("gpsimd", dbg_hi[k2, :, :], SG0[:, 0:1024], w=(f"dhi_{k2}",), r=("sg0",))
            vop(lambda h, k2=k2: h.tensor_copy(SG0[:, 0:1024],
                                               SB["bfC"][:, k2 * NTOK:(k2 + 1) * NTOK]),
                r=(f"lo_{k2}", "sg0"), w=("sg0",))
            dma("gpsimd", dbg_lo[k2, :, :], SG0[:, 0:1024], w=(f"dlo_{k2}",), r=("sg0",))
        for i2 in range(8):
            vop(lambda h, i2=i2: h.tensor_copy(SG0,
                                               SB["bfA"][:, i2 * 2048:(i2 + 1) * 2048]),
                r=HITM + ("sg0",), w=("sg0",))
            dma("gpsimd", dbg_hitm[i2, :, :], SG0, w=(f"dhitm_{i2}",), r=("sg0",))
            vop(lambda h, i2=i2: h.tensor_copy(SG0,
                                               SB["bfB"][:, i2 * 2048:(i2 + 1) * 2048]),
                r=LOTM + ("sg0",), w=("sg0",))
            dma("gpsimd", dbg_lotm[i2, :, :], SG0, w=(f"dlotm_{i2}",), r=("sg0",))

    for i in range(8):
        src = x_in.rearrange("(i p) d -> p i d", p=128)[:, i, :]
        dma("scalar", SG0, src, w=("sg0",))
        vop(lambda h, i=i: h.tensor_tensor(out=SG0, in0=SG0,
                                           in1=SB["bfA"][:, i * 2048:(i + 1) * 2048],
                                           op=ALU.add),
            r=("sg0",) + HITM, w=("sg0",))
        vop(lambda h, i=i: h.tensor_tensor(out=SG0, in0=SG0,
                                           in1=SB["bfB"][:, i * 2048:(i + 1) * 2048],
                                           op=ALU.add),
            r=("sg0",) + LOTM, w=("sg0",))
        if STAGE == 1:
            dma("gpsimd", out_t.rearrange("(i p) d -> p i d", p=128)[:, i, :], SG0,
                w=(f"xout_{i}",), r=("sg0",))
        else:
            dma("gpsimd", x1_dr.rearrange("(i p) d -> p i d", p=128)[:, i, :], SG0,
                w=(f"x1dr_{i}",), r=("sg0",))

    def final_waits(engs=("gpsimd", "scalar")):
        for eng in engs:
            n = P.dmacnt[eng]
            waits = []
            for j in range(Prog.NDSEM):
                uses = (n - 1 - j) // Prog.NDSEM + 1 if n > j else 0
                if uses > 0:
                    waits.append((f"D_{eng}_{j}", uses * 16))
            P.ops[eng].append((waits, lambda h: h.nop(), (f"E_{eng}", 1)))
            P.cnt[eng] += 1

    if STAGE == 1:
        final_waits(("gpsimd",))
    else:
        X1K = tuple(f"x1dr_{i}" for i in range(8))
        P.retag(tuple(f"lo_{k}" for k in range(16)) + ("bfC_lotm",), ("bfCa",))
        P.retag(tuple(f"hi_{k}" for k in range(16)) + ("outb",), ("bigAfm",))
        for i in range(8):
            src = x1_dr.rearrange("(i p) d -> p i d", p=128)[:, i, :]
            ln_tile(i, src, X1K, "l1", "bfC", aq=False, tern=True,
                    xn_arena=("bfCa",), tq_arena=("bigAfm",))
        XN1 = tuple(f"xnfm_l1_{i}" for i in range(8))
        TQ1 = tuple(f"tqfm_l1_{i}" for i in range(8))
        TQB = SB["bigA"][:].bitcast(BF16)

        P.retag(HITM + ("bfA_hi",), ("bfAa",))
        P.retag(LOTM + ("bfB_lotm", "tb"), ("bfBa",))

        def xbc_slot(j):
            if j < 15:
                return SB["bfA"], j * 1027, "bfAa"
            if j < 30:
                return SB["bfB"], (j - 15) * 1027, "bfBa"
            return TQB, 16384 + (j - 30) * 1027, "bigAfm"

        for og in range(17):
            cg0 = og * 4
            cgn = min(4, 67 - cg0)
            ncol = 512 if og < 16 else 320
            for k in range(KC):
                wb, wkey, q = wtile()
                dma(q, wb[:, 0:ncol],
                    Win[k * 128:(k + 1) * 128, cg0 * 128:cg0 * 128 + ncol], w=(wkey,))
                for c in range(cgn):
                    mw = 128 if cg0 + c < 66 else 64
                    for nt in range(2):
                        pk = f"ps{2 * c + nt}"
                        rk = (wkey,) + (XN1 if k in (0, KC - 1) else ())
                        top(lambda h, k=k, c=c, nt=nt, wb=wb, mw=mw:
                            h.matmul(PS[2 * c + nt][0:mw, 0:512],
                                     wb[:, c * 128:c * 128 + mw],
                                     SB["bfC"][:, k * NTOK + nt * 512:
                                               k * NTOK + (nt + 1) * 512],
                                     start=(k == 0), stop=(k == KC - 1)),
                            r=rk if k in (0, KC - 1) else (wkey,),
                            w=(pk,) if k in (0, KC - 1) else ())
            for c in range(cgn):
                oc = cg0 + c
                for nt in range(2):
                    pk = f"ps{2 * c + nt}"
                    if oc < 32:
                        aop(lambda h, c=c, nt=nt: h.activation(
                                out=SB["smb"][:, nt * 512:(nt + 1) * 512],
                                in_=PS[2 * c + nt][:, 0:512], func=AF.Silu),
                            r=(pk, "smb0"), w=("smb0",))
                    elif oc < 66:
                        aop(lambda h, c=c, nt=nt: h.activation(
                                out=SB["smb"][:, nt * 512:(nt + 1) * 512],
                                in_=PS[2 * c + nt][:, 0:512], func=AF.Identity),
                            r=(pk, "smb0"), w=("smb0",))
                    else:
                        aop(lambda h, c=c, nt=nt: h.activation(
                                out=SB["pool"][0:NH, nt * 512:(nt + 1) * 512],
                                in_=PS[2 * c + nt][0:NH, 0:512], func=AF.Identity),
                            r=(pk, "pool_dt"), w=("pool_dt",))
                if oc < 32:
                    dma("scalar", gate_dr[oc, :, :], SB["smb"][:, 0:1024],
                        w=(f"gatedr_{oc}",), r=("smb0",))
                elif oc < 66:
                    j = oc - 32
                    buf, base, akey = xbc_slot(j)
                    dma("sync", buf[:, base + 3:base + 1027], SB["smb"][:, 0:1024],
                        w=(f"xbc_{j}", akey), r=("smb0",))

        XBC = tuple(f"xbc_{j}" for j in range(34))

        for j in range(34):
            buf, base, akey = xbc_slot(j)
            dma("gpsimd", hal_bi[:, j * 3:(j + 1) * 3], buf[:, base + 1024:base + 1027],
                w=("halbi",), r=(f"xbc_{j}",))
        if SKIP_CC:
            for rnk in range(4):
                dma("gpsimd", hal_bo[rnk * 128:(rnk + 1) * 128, :], hal_bi,
                    w=("halbo",), r=("halbi",))
        else:
            P.op("gpsimd", lambda h: h.collective_compute(
                    "AllGather", ALU.bypass, replica_groups=[[0, 1, 2, 3], [4, 5, 6, 7]],
                    ins=[hal_bi.opt()], outs=[hal_bo.opt()]),
                reads=("halbi",), writes=("halbo",), is_cc=True)
        WRT = SB["wst"][:, 4096:6144]
        WRS = SB["wst"][:, 6144:8192]
        for oc in range(16):
            dma("sync", SG0, Wf[oc * 128:(oc + 1) * 128, :], w=("sg0",))
            dma("gpsimd", SB["smb"][:, 0:2048], Rf[oc * 128:(oc + 1) * 128, :],
                w=("smb0",))
            aop(lambda h: h.activation(out=WRS, in_=SG0, func=AF.Sign,
                                       bias=cc_(267, 268)),
                r=("sg0", "wst45", "c_b3"), w=("wst45",))
            aop(lambda h: h.activation(out=WRT, in_=SG0, func=AF.Sign,
                                       bias=cc_(268, 269)),
                r=("sg0", "wst45", "c_b4"), w=("wst45",))
            vop(lambda h: h.tensor_tensor(out=WRS, in0=WRS, in1=WRT, op=ALU.add),
                r=("wst45",), w=("wst45",))
            vop(lambda h: h.scalar_tensor_tensor(out=WRS, in0=WRS, scalar=0.3,
                                                 in1=SB["smb"][:, 0:2048],
                                                 op0=ALU.mult, op1=ALU.add),
                r=("wst45", "smb0"), w=("wst45",))
            dmat(WRT.rearrange("p (k f) -> p k f", f=128),
                 WRS.rearrange("p (k f) -> p k f", f=128),
                 w=("wst45",), r=("wst45",))
            for k in range(KC):
                for nt in range(2):
                    top(lambda h, k=k, nt=nt: h.matmul(
                            PS[nt][:, 0:512], WRT[:, k * 128:(k + 1) * 128],
                            TQB[:, k * NTOK + nt * 512:k * NTOK + (nt + 1) * 512],
                            start=(k == 0), stop=(k == KC - 1)),
                        r=("wst45",) + (TQ1 if k in (0, KC - 1) else ()),
                        w=(f"ps{nt}",) if k in (0, KC - 1) else ())
            for nt in range(2):
                aop(lambda h, nt=nt: h.activation(out=SB["smb"][:, nt * 512:(nt + 1) * 512],
                                                  in_=PS[nt][:, 0:512], func=AF.Silu,
                                                  scale=0.25),
                    r=(f"ps{nt}", "smb0"), w=("smb0",))
            dma("scalar", g2_dr[oc, :, :], SB["smb"][:, 0:1024], w=(f"g2dr_{oc}",),
                r=("smb0",))

        vop(lambda h: h.memset(SB["smb"].bitcast(F32)[:, 0:102], 0.0),
            r=("smb0",), w=("smb0",))
        for rnk in range(4):
            dma("gpsimd", SB["smb"].bitcast(F32)[:, 128:230],
                hal_bo[rnk * 128:(rnk + 1) * 128, :], w=("smbh", "smb0"),
                r=("halbo",))
            vop(lambda h, rnk=rnk: h.scalar_tensor_tensor(
                    out=SB["smb"].bitcast(F32)[:, 0:102],
                    in0=SB["smb"].bitcast(F32)[:, 128:230],
                    scalar=CSB[:, 544 + rnk:545 + rnk],
                    in1=SB["smb"].bitcast(F32)[:, 0:102], op0=ALU.mult, op1=ALU.add),
                r=("smbh", "hsel", "smb0"), w=("smb0",))
        for j in range(34):
            buf, base, akey = xbc_slot(j)
            vop(lambda h, j=j, buf=buf, base=base: h.tensor_copy(
                    buf[:, base:base + 3], SB["smb"].bitcast(F32)[:, j * 3:j * 3 + 3]),
                r=("smb0",), w=(f"xbc_{j}", akey))

        for j in range(34):
            buf, base, akey = xbc_slot(j)
            for half in (1, 0):
                c0 = base + 3 + half * 512
                acc = SG0[:, 0:512]
                vop(lambda h, buf=buf, c0=c0, j=j: h.tensor_scalar(
                        out=acc, in0=buf[:, c0 - 3:c0 + 509],
                        scalar1=CSB[:, 304 + j * 4:305 + j * 4], scalar2=None,
                        op0=ALU.mult),
                    r=(f"xbc_{j}", "convw", "sg0"), w=("sg0",))
                for kk in range(1, 4):
                    vop(lambda h, buf=buf, c0=c0, j=j, kk=kk: h.scalar_tensor_tensor(
                            out=acc, in0=buf[:, c0 - 3 + kk:c0 + 509 + kk],
                            scalar=CSB[:, 304 + j * 4 + kk:305 + j * 4 + kk],
                            in1=acc, op0=ALU.mult, op1=ALU.add),
                        r=(f"xbc_{j}", "convw", "sg0"), w=("sg0",))
                aop(lambda h, buf=buf, c0=c0, j=j: h.activation(
                        out=buf[:, c0:c0 + 512], in_=acc, func=AF.Silu,
                        bias=CSB[:, 224 + j:225 + j]),
                    r=("sg0", "convb"), w=(f"xbc_{j}",))

        def xs(j, c0, w_):
            buf, base, akey = xbc_slot(j)
            return buf[:, base + 3 + c0:base + 3 + c0 + w_]

        PD = SB["pool"]
        DTT = PD[0:NH, 0:1024]
        CAA = PD[0:NH, 1024:2048]
        SCR = SB["sg"][0:NH, 0:1024]
        aop(lambda h: h.activation(out=SCR, in_=DTT, func=AF.Abs,
                                   bias=CSB[0:NH, 258:259]),
            r=("pool_dt", "dtbias", "sg0"), w=("sg0",))
        aop(lambda h: h.activation(out=SCR, in_=SCR, func=AF.Exp, scale=-1.0),
            r=("sg0",), w=("sg0",))
        aop(lambda h: h.activation(out=SCR, in_=SCR, func=AF.Ln,
                                   bias=CSB[0:NH, 262:263]),
            r=("sg0", "c_one"), w=("sg0",))
        vop(lambda h: h.tensor_scalar(out=CAA, in0=DTT,
                                      scalar1=CSB[0:NH, 258:259], scalar2=0.0,
                                      op0=ALU.add, op1=ALU.max),
            r=("pool_dt", "dtbias", "pool_ca"), w=("pool_ca",))
        vop(lambda h: h.tensor_tensor(out=DTT, in0=CAA, in1=SCR, op=ALU.add),
            r=("pool_ca", "sg0", "pool_dt"), w=("pool_dt",))
        vop(lambda h: h.tensor_copy(SB["sm64"][:, 0:1024], DTT),
            r=("pool_dt", "sm64"), w=("sm64",))
        vop(lambda h: h.memset(CSB[0:NH, 576:704], 1.0), r=("stc",), w=("stc",))
        vop(lambda h: h.tensor_scalar(out=CAA, in0=DTT, scalar1=CSB[0:NH, 259:260],
                                      scalar2=None, op0=ALU.mult),
            r=("pool_dt", "Acol", "pool_ca"), w=("pool_ca",))
        for c in range(NCH):
            vop(lambda h, c=c: h.tensor_tensor_scan(
                    out=CAA[:, c * 128:(c + 1) * 128],
                    data0=CSB[0:NH, 576:704], data1=CAA[:, c * 128:(c + 1) * 128],
                    initial=0.0, op0=ALU.mult, op1=ALU.add),
                r=("stc", "pool_ca"), w=("pool_ca",))
        for c in range(NCH):
            vop(lambda h, c=c: h.tensor_copy(CSB[0:NH, 512 + c:513 + c],
                                             CAA[:, c * 128 + 127:c * 128 + 128]),
                r=("pool_ca", "stc"), w=("stc",))
        vop(lambda h: h.memset(CSB[0:NH, 528:529], 0.0), r=("stc",), w=("stc",))
        for c in range(1, NCH):
            vop(lambda h, c=c: h.tensor_tensor(out=CSB[0:NH, 528 + c:529 + c],
                                               in0=CSB[0:NH, 527 + c:528 + c],
                                               in1=CSB[0:NH, 511 + c:512 + c],
                                               op=ALU.add),
                r=("stc",), w=("stc",))
        for c in range(NCH):
            aop(lambda h, c=c: h.activation(out=SB["sm64"][:, 1024 + c * 128:
                                                           1024 + (c + 1) * 128],
                                            in_=CAA[:, c * 128:(c + 1) * 128],
                                            func=AF.Exp, scale=-1.0,
                                            bias=CSB[0:NH, 512 + c:513 + c]),
                r=("pool_ca", "stc", "sm64"), w=("sm64",))
        vop(lambda h: h.tensor_tensor(out=SB["sm64"][:, 1024:2048],
                                      in0=SB["sm64"][:, 1024:2048],
                                      in1=SB["sm64"][:, 0:1024], op=ALU.mult),
            r=("sm64",), w=("sm64",))
        for c in range(NCH):
            top(lambda h, c=c: h.transpose(PS[6][:, 0:64],
                                           CAA[:, c * 128:(c + 1) * 128],
                                           SB["eye"][:]),
                r=("pool_ca", "eye", "ps6"), w=("ps6",))
            vop(lambda h, c=c: h.tensor_copy(PD[:, 2048 + c * 64:2048 + (c + 1) * 64],
                                             PS[6][:, 0:64]),
                r=("ps6", "pool_cat"), w=("pool_cat",))
        aop(lambda h: h.activation(out=PD[:, 2560:3072], in_=PD[:, 2048:2560],
                                   func=AF.Exp),
            r=("pool_cat", "pool_ecat"), w=("pool_ecat",))

        BGA = SB["bigA"][:]
        P.retag(tuple(f"wt{j}" for j in range(NWSL)), ("rows",))
        ROWS = SB["wst"][0:1, 0:8192].bitcast(F32)
        lc_ap = dscr("lc_dr", (NCH, 128, INTER))
        vop(lambda h: h.memset(BGA[:, 0:4096], 0.0), r=("bigAfm",), w=("carry",))
        P.retag(XN1, ("bfCb",))
        for c in range(NCH):
            dma("scalar", lc_ap[c, :, :], BGA[:, 0:4096], w=(f"lc_{c}",), r=("carry",))
            for j in range(32):
                top(lambda h, j=j, c=c: h.matmul(
                        PS[j % 4][:, 0:128], SB["gsel"][0:NH, j * 128:(j + 1) * 128],
                        SB["sm64"][:, 1024 + c * 128:1024 + (c + 1) * 128],
                        start=True, stop=True),
                    r=("gsel", "sm64", f"ps{j % 4}"), w=(f"ps{j % 4}",))
                vop(lambda h, j=j, c=c: h.tensor_tensor(
                        out=SB["bfC"][:, j * 128:(j + 1) * 128],
                        in0=xs(j, c * 128, 128), in1=PS[j % 4][:, 0:128], op=ALU.mult),
                    r=(f"xbc_{j}", f"ps{j % 4}", "bfCb"), w=("bfCb",))
            dmat(SB["bfC"][:, 4096:8192].rearrange("p (k f) -> p k f", f=128),
                 SB["bfC"][:, 0:4096].rearrange("p (k f) -> p k f", f=128),
                 w=("bfCb",), r=("bfCb",))
            dmat(SB["smb"][:, 0:128].rearrange("p (k f) -> p k f", f=128),
                 xs(32, c * 128, 128).rearrange("p (k f) -> p k f", f=128),
                 w=("smb0",), r=("xbc_32",))
            aop(lambda h, c=c: h.activation(out=CSB[0:NH, 560:561],
                                            in_=CSB[0:NH, 512 + c:513 + c],
                                            func=AF.Exp),
                r=("stc",), w=("stc",))
            dma("sync", SG0[0:1, 1984:2048], CSB[0:NH, 560:561], w=("sg0",),
                r=("stc",))
            top(lambda h: h.matmul(PS[6][:, 0:64], CSB[0:1, 704:832],
                                   SG0[0:1, 1984:2048], start=True, stop=True),
                r=("onesr", "sg0", "ps6"), w=("ps6",))
            vop(lambda h: h.tensor_copy(SG1[:, 1024:1088], PS[6][:, 0:64]),
                r=("ps6", "sg1"), w=("sg1",))
            for g in range(8):
                top(lambda h, g=g: h.matmul(
                        PS[4 + g % 2][:, 0:512], SB["smb"][:, 0:128],
                        SB["bfC"][:, 4096 + g * 512:4096 + (g + 1) * 512],
                        start=True, stop=True),
                    r=("smb0", "bfCb", f"ps{4 + g % 2}"), w=(f"ps{4 + g % 2}",))
                for hh2 in range(8):
                    hida = g * 8 + hh2
                    vop(lambda h, g=g, hh2=hh2, hida=hida: h.scalar_tensor_tensor(
                            out=BGA[:, hida * 64:(hida + 1) * 64],
                            in0=BGA[:, hida * 64:(hida + 1) * 64],
                            scalar=SG1[:, 1024 + hida:1025 + hida],
                            in1=PS[4 + g % 2][:, hh2 * 64:(hh2 + 1) * 64],
                            op0=ALU.mult, op1=ALU.add),
                        r=("sg1", f"ps{4 + g % 2}", "carry"), w=("carry",))

        vop(lambda h: h.tensor_tensor(out=CSB[0:NH, 561:562],
                                      in0=CSB[0:NH, 535:536],
                                      in1=CSB[0:NH, 519:520], op=ALU.add),
            r=("stc",), w=("stc",))
        aop(lambda h: h.activation(out=CSB[0:NH, 561:562], in_=CSB[0:NH, 561:562],
                                   func=AF.Exp),
            r=("stc",), w=("stc",))
        dma("sync", SG0[0:1, 1984:2048], CSB[0:NH, 561:562], w=("sg0",), r=("stc",))
        top(lambda h: h.matmul(PS[6][:, 0:64], CSB[0:1, 704:832], SG0[0:1, 1984:2048],
                               start=True, stop=True),
            r=("onesr", "sg0", "ps6"), w=("ps6",))
        vop(lambda h: h.tensor_copy(SG1[:, 448:512], PS[6][:, 0:64]),
            r=("ps6", "sg1"), w=("sg1",))
        dma("gpsimd", ssd_bi[:, 0:4096], BGA[:, 0:4096], w=("ssdbi",), r=("carry",))
        dma("gpsimd", ssd_bi[:, 4096:4160], SG1[:, 448:512], w=("ssdbi",), r=("sg1",))
        if SKIP_CC:
            for rnk in range(4):
                dma("gpsimd", ssd_bo[rnk * 128:(rnk + 1) * 128, :], ssd_bi,
                    w=("ssdbo",), r=("ssdbi",))
        else:
            P.op("gpsimd", lambda h: h.collective_compute(
                    "AllGather", ALU.bypass, replica_groups=[[0, 1, 2, 3], [4, 5, 6, 7]],
                    ins=[ssd_bi.opt()], outs=[ssd_bo.opt()]),
                reads=("ssdbi",), writes=("ssdbo",), is_cc=True)
        BFCF = SB["bfC"][:].bitcast(F32)
        P.retag(("bfCb",), ("bfCf",))
        vop(lambda h: h.memset(BGA[:, 4096:8192], 0.0), r=("cin",), w=("cin",))
        vop(lambda h: h.memset(BFCF[:, 1216:1280], 1.0), r=("bfCf",), w=("bfCf",))
        for kk in range(1, 4):
            dma("gpsimd", BFCF[:, 1536:1600], ssd_bo[kk * 128:(kk + 1) * 128, 4096:4160],
                w=("bfCf",), r=("bfCf", "ssdbo")),
            vop(lambda h, kk=kk: h.tensor_scalar(
                    out=BFCF[:, 1280 + kk * 64:1280 + (kk + 1) * 64],
                    in0=BFCF[:, 1536:1600], scalar1=-1.0, scalar2=None, op0=ALU.add),
                r=("bfCf",), w=("bfCf",))
            vop(lambda h, kk=kk: h.tensor_scalar(
                    out=BFCF[:, 1280 + kk * 64:1280 + (kk + 1) * 64],
                    in0=BFCF[:, 1280 + kk * 64:1280 + (kk + 1) * 64],
                    scalar1=CSB[:, 548 + kk:549 + kk], scalar2=1.0,
                    op0=ALU.mult, op1=ALU.add),
                r=("bfCf", "ssel"), w=("bfCf",))
        for rr in (2, 1, 0):
            vop(lambda h, rr=rr: h.tensor_tensor(
                    out=BFCF[:, 1024 + rr * 64:1024 + (rr + 1) * 64],
                    in0=BFCF[:, 1024 + (rr + 1) * 64:1024 + (rr + 2) * 64],
                    in1=BFCF[:, 1280 + (rr + 1) * 64:1280 + (rr + 2) * 64], op=ALU.mult),
                r=("bfCf",), w=("bfCf",))
        for rr in range(4):
            vop(lambda h, rr=rr: h.tensor_scalar(
                    out=BFCF[:, 1024 + rr * 64:1024 + (rr + 1) * 64],
                    in0=BFCF[:, 1024 + rr * 64:1024 + (rr + 1) * 64],
                    scalar1=CSB[:, 548 + rr:549 + rr], scalar2=None, op0=ALU.mult),
                r=("bfCf", "ssel"), w=("bfCf",))
        for rr in range(4):
            dma("gpsimd", SB["sg"][:, 0:4096], ssd_bo[rr * 128:(rr + 1) * 128, 0:4096],
                w=("sg0", "sg1"), r=("ssdbo",)),
            for g in range(8):
                vop(lambda h, rr=rr, g=g: h.tensor_copy(
                        BFCF[:, 0:512].rearrange("p (a b) -> p a b", b=64),
                        BFCF[:, 1024 + rr * 64 + g * 8:1024 + rr * 64 + (g + 1) * 8]
                        .unsqueeze(2).broadcast_to([128, 8, 64])),
                    r=("bfCf",), w=("bfCf",))
                vop(lambda h, g=g: h.tensor_tensor(
                        out=BFCF[:, 512:1024],
                        in0=SB["sg"][:, g * 512:(g + 1) * 512],
                        in1=BFCF[:, 0:512], op=ALU.mult),
                    r=("sg0", "sg1", "bfCf"), w=("bfCf",))
                vop(lambda h, g=g: h.tensor_tensor(
                        out=BGA[:, 4096 + g * 512:4096 + (g + 1) * 512],
                        in0=BGA[:, 4096 + g * 512:4096 + (g + 1) * 512],
                        in1=BFCF[:, 512:1024], op=ALU.add),
                    r=("bfCf", "cin"), w=("cin",))

        P.retag(("bfCf",), ("bfCb",))
        for c in range(NCH):
            for j in range(32):
                top(lambda h, j=j, c=c: h.matmul(
                        PS[j % 4][:, 0:128], SB["gsel"][0:NH, j * 128:(j + 1) * 128],
                        SB["sm64"][:, c * 128:(c + 1) * 128], start=True, stop=True),
                    r=("gsel", "sm64", f"ps{j % 4}"), w=(f"ps{j % 4}",))
                vop(lambda h, j=j, c=c: h.tensor_tensor(
                        out=SB["bfC"][:, j * 128:(j + 1) * 128],
                        in0=xs(j, c * 128, 128), in1=PS[j % 4][:, 0:128], op=ALU.mult),
                    r=(f"xbc_{j}", f"ps{j % 4}", "bfCb"), w=("bfCb",))
            dmat(SB["bfC"][:, 4096:8192].rearrange("p (k f) -> p k f", f=128),
                 SB["bfC"][:, 0:4096].rearrange("p (k f) -> p k f", f=128),
                 w=("bfCb",), r=("bfCb",))
            top(lambda h, c=c: h.matmul(PS[2][:, 0:128], xs(32, c * 128, 128),
                                        xs(33, c * 128, 128), start=True, stop=True),
                r=("xbc_32", "xbc_33", "ps2"), w=("ps2",))
            vop(lambda h: h.tensor_tensor(out=SB["smb"][:, 0:128], in0=PS[2][:, 0:128],
                                          in1=SB["triu"][:], op=ALU.mult),
                r=("ps2", "triu", "smb0"), w=("smb0",))
            for half in range(2):
                dma("sync", ROWS[:, 0:4096],
                    PD[half * 32:(half + 1) * 32, 1024 + c * 128:1024 + (c + 1) * 128],
                    w=("rows",), r=("pool_ca",))
                for sub in range(8):
                    top(lambda h, sub=sub: h.matmul(
                            PS[4 + sub % 2][:, 0:512], CSB[0:1, 704:832],
                            ROWS[:, sub * 512:(sub + 1) * 512],
                            start=True, stop=True),
                        r=("onesr", "rows", f"ps{4 + sub % 2}"), w=(f"ps{4 + sub % 2}",))
                    for hh2 in range(4):
                        hida = half * 32 + sub * 4 + hh2
                        vop(lambda h, sub=sub, hh2=hh2, hida=hida, c=c:
                            h.scalar_tensor_tensor(
                                out=SG0[:, hh2 * 128:(hh2 + 1) * 128],
                                in0=PS[4 + sub % 2][:, hh2 * 128:(hh2 + 1) * 128],
                                scalar=PD[:, 2048 + c * 64 + hida:2049 + c * 64 + hida],
                                in1=CSB[:, 261:262].broadcast_to([128, 128]),
                                op0=ALU.subtract, op1=ALU.min),
                            r=(f"ps{4 + sub % 2}", "pool_cat", "c_zero", "sg0"),
                            w=("sg0",))
                    aop(lambda h, half=half, sub=sub: h.activation(
                            out=SB["bfC"][:, 8192 + half * 4096 + sub * 512:
                                          8192 + half * 4096 + (sub + 1) * 512],
                            in_=SG0[:, 0:512], func=AF.Exp),
                        r=("sg0", "bfCb"), w=("bfCb",))
            vop(lambda h: h.tensor_tensor(
                    out=SB["bfC"][:, 8192:16384].rearrange("p (a b) -> p a b", b=128),
                    in0=SB["bfC"][:, 8192:16384].rearrange("p (a b) -> p a b", b=128),
                    in1=SB["smb"][:, 0:128].unsqueeze(1).broadcast_to([128, 64, 128]),
                    op=ALU.mult),
                r=("smb0", "bfCb"), w=("bfCb",))
            aop(lambda h, c=c: h.activation(out=CSB[0:NH, 562:563],
                                            in_=CSB[0:NH, 528 + c:529 + c],
                                            func=AF.Exp),
                r=("stc",), w=("stc",))
            dma("sync", SG0[0:1, 1984:2048], CSB[0:NH, 562:563], w=("sg0",),
                r=("stc",))
            dma("scalar", BGA[:, 0:4096], lc_ap[c, :, :], w=("carry",),
                r=(f"lc_{c}",))
            top(lambda h: h.matmul(PS[4][:, 0:64], CSB[0:1, 704:832],
                                   SG0[0:1, 1984:2048], start=True, stop=True),
                r=("onesr", "sg0", "ps4"), w=("ps4",))
            vop(lambda h: h.tensor_copy(SG1[:, 1024:1088], PS[4][:, 0:64]),
                r=("ps4", "sg1"), w=("sg1",))
            for g in range(8):
                vop(lambda h, g=g: h.tensor_copy(
                        SG0[:, 0:512].rearrange("p (a b) -> p a b", b=64),
                        SG1[:, 1024 + g * 8:1024 + (g + 1) * 8].unsqueeze(2)
                        .broadcast_to([128, 8, 64])),
                    r=("sg1", "sg0"), w=("sg0",))
                vop(lambda h, g=g: h.tensor_tensor(
                        out=SG0[:, 512:1024],
                        in0=BGA[:, 4096 + g * 512:4096 + (g + 1) * 512],
                        in1=SG0[:, 0:512], op=ALU.mult),
                    r=("cin", "sg0"), w=("sg0",))
                vop(lambda h, g=g: h.tensor_tensor(
                        out=SB["bfC"][:, g * 512:(g + 1) * 512],
                        in0=SG0[:, 512:1024],
                        in1=BGA[:, g * 512:(g + 1) * 512], op=ALU.add),
                    r=("sg0", "carry", "bfCb"), w=("bfCb",))
            for g in range(8):
                top(lambda h, g=g, c=c: h.matmul(
                        PS[2 + g % 2][:, 0:512], xs(33, c * 128, 128),
                        SB["bfC"][:, g * 512:(g + 1) * 512], start=True, stop=True),
                    r=("xbc_33", "bfCb", f"ps{2 + g % 2}"), w=(f"ps{2 + g % 2}",))
                for hh2 in range(8):
                    hida = g * 8 + hh2
                    top(lambda h, g=g, hida=hida, hh2=hh2: h.matmul(
                            PS[6 + g % 2][:, hh2 * 64:(hh2 + 1) * 64],
                            SB["bfC"][:, 8192 + hida * 128:8192 + (hida + 1) * 128],
                            SB["bfC"][:, 4096 + (hida // 2) * 128 + (hida % 2) * 64:
                                      4096 + (hida // 2) * 128 + (hida % 2) * 64 + 64],
                            start=(hh2 == 0), stop=(hh2 == 7)),
                        r=("bfCb", f"ps{6 + g % 2}") if hh2 in (0, 7) else ("bfCb",),
                        w=(f"ps{6 + g % 2}",) if hh2 in (0, 7) else ())
                vop(lambda h, g=g: h.tensor_copy(
                        SB["bfC"][:, g * 512:(g + 1) * 512], PS[6 + g % 2][:, 0:512]),
                    r=(f"ps{6 + g % 2}", "bfCb"), w=("bfCb",))
                for hh2 in range(8):
                    hida = g * 8 + hh2
                    vop(lambda h, g=g, hida=hida, hh2=hh2, c=c: h.scalar_tensor_tensor(
                            out=SB["bfC"][:, g * 512 + hh2 * 64:g * 512 + (hh2 + 1) * 64],
                            in0=PS[2 + g % 2][:, hh2 * 64:(hh2 + 1) * 64],
                            scalar=PD[:, 2560 + c * 64 + hida:2561 + c * 64 + hida],
                            in1=SB["bfC"][:, g * 512 + hh2 * 64:g * 512 + (hh2 + 1) * 64],
                            op0=ALU.mult, op1=ALU.add),
                        r=(f"ps{2 + g % 2}", "pool_ecat", "bfCb"),
                        w=("bfCb",))
            dmat(SB["bfC"][:, 4096:8192].rearrange("p (k f) -> p k f", f=128),
                 SB["bfC"][:, 0:4096].rearrange("p (k f) -> p k f", f=128),
                 w=("bfCb",), r=("bfCb",))
            for j in range(32):
                vop(lambda h, j=j, c=c: h.scalar_tensor_tensor(
                        out=SB["bfC"][:, 4096 + j * 128:4096 + (j + 1) * 128],
                        in0=xs(j, c * 128, 128), scalar=CSB[:, 440 + j:441 + j],
                        in1=SB["bfC"][:, 4096 + j * 128:4096 + (j + 1) * 128],
                        op0=ALU.mult, op1=ALU.add),
                    r=(f"xbc_{j}", "dcolb", "bfCb"), w=("bfCb",))
            dma("scalar", SB["bfC"][:, 8192:12288].rearrange("p (a b) -> p a b", b=128),
                gate_dr.rearrange("a p b -> p a b")[:, :, c * 128:(c + 1) * 128],
                w=("bfCb",), r=tuple(f"gatedr_{oc}" for oc in range(32)) + ("bfCb",))
            aop(lambda h: h.activation(out=SB["bfC"][:, 8192:12288],
                                       in_=SB["bfC"][:, 8192:12288], func=AF.Silu),
                r=("bfCb",), w=("bfCb",))
            vop(lambda h: h.tensor_tensor(out=SB["bfC"][:, 8192:12288],
                                          in0=SB["bfC"][:, 8192:12288],
                                          in1=SB["bfC"][:, 4096:8192], op=ALU.mult),
                r=("bfCb",), w=("bfCb",))
            aop(lambda h: h.activation(out=SB["bfC"][:, 12288:16384],
                                       in_=SB["bfC"][:, 8192:12288], func=AF.Square),
                r=("bfCb",), w=("bfCb",))
            for sub in range(8):
                top(lambda h, sub=sub: h.matmul(
                        PS[4][0:1, 0:512], SB["triu"][:, 127:128],
                        SB["bfC"][:, 12288 + sub * 512:12288 + (sub + 1) * 512],
                        start=(sub == 0), stop=(sub == 7)),
                    r=("triu", "bfCb", "ps4") if sub == 0 else ("triu", "bfCb"),
                    w=("ps4",) if sub in (0, 7) else ())
            vop(lambda h: h.tensor_copy(SG0[0:1, 0:512], PS[4][0:1, 0:512]),
                r=("ps4", "sg0"), w=("sg0",))
            vop(lambda h: h.reduce_sum(
                    out=SG0[0:1, 512:640].unsqueeze(2),
                    in_=SG0[0:1, 0:512].rearrange("p (a b) -> p b a", b=128),
                    axis=AX.X),
                r=("sg0",), w=("sg0",))
            vop(lambda h: h.tensor_scalar(out=SG0[0:1, 512:640], in0=SG0[0:1, 512:640],
                                          scalar1=1.0 / INTER, scalar2=GEPS,
                                          op0=ALU.mult, op1=ALU.add),
                r=("sg0",), w=("sg0",))
            aop(lambda h: h.activation(out=SG0[0:1, 512:640], in_=SG0[0:1, 512:640],
                                       func=AF.Sqrt),
                r=("sg0",), w=("sg0",))
            vop(lambda h: h.reciprocal(out=SG0[0:1, 512:640], in_=SG0[0:1, 512:640]),
                r=("sg0",), w=("sg0",))
            top(lambda h: h.matmul(PS[5][:, 0:128], CSB[0:1, 704:832], SG0[0:1, 512:640],
                                   start=True, stop=True),
                r=("onesr", "sg0", "ps5"), w=("ps5",))
            vop(lambda h: h.tensor_copy(SG0[:, 1024:1152], PS[5][:, 0:128]),
                r=("ps5", "sg0"), w=("sg0",))
            vop(lambda h: h.tensor_tensor(
                    out=SB["bfC"][:, 8192:12288].rearrange("p (a b) -> p a b", b=128),
                    in0=SB["bfC"][:, 8192:12288].rearrange("p (a b) -> p a b", b=128),
                    in1=SG0[:, 1024:1152].unsqueeze(1).broadcast_to([128, 32, 128]),
                    op=ALU.mult),
                r=("sg0", "bfCb"), w=("bfCb",))
            dma("scalar", z_dr.rearrange("a p b -> p a b")[:, :, c * 128:(c + 1) * 128],
                SB["bfC"][:, 8192:12288].rearrange("p (a b) -> p a b", b=128),
                w=(f"zdr_{c}",), r=("bfCb",))

        P.retag(("carry", "cin", "spare", "rows")
                + tuple(f"xbc_{j}" for j in range(30, 34)) + ("bigAfm",),
                ("bigAfree",))
        P.retag(tuple(f"xbc_{j}" for j in range(15)) + ("bfAa",), ("bfAw",))
        P.retag(tuple(f"xbc_{j}" for j in range(15, 30)) + ("bfBa",), ("bfBz",))
        P.retag(("bfCb",), ("bfCz",))
        WOB = SB["bfA"][:]
        ZR = [SB["bfB"][:], SB["bfC"][:]]
        ZKEY = ["bfBz", "bfCz"]
        OUT2 = SB["bigA"][:].bitcast(BF16)
        for tt in range(2):
            dma("scalar" if tt == 0 else "sync",
                ZR[tt][:, 0:16384].rearrange("p (a b) -> p a b", b=512),
                z_dr.rearrange("a p b -> p a b")[:, :, tt * 512:(tt + 1) * 512],
                w=(ZKEY[tt],),
                r=tuple(f"zdr_{c2}" for c2 in range(NCH)) + (ZKEY[tt],))
        P.retag(("sg0",), ("sgq0", "sgq1", "sgq2", "sgq3"))
        WOQ = ("gpsimd", "sync", "scalar", "gpsimd")
        for eq in range(4):
            for kb in range(32):
                sq = kb % 4
                dma(WOQ[sq], SG0[:, sq * 512:(sq + 1) * 512],
                    Wout[kb * 128:(kb + 1) * 128, eq * 512:(eq + 1) * 512],
                    w=(f"sgq{sq}",), r=()),
                aop(lambda h, kb=kb, sq=sq: h.activation(
                        out=WOB[:, kb * 512:(kb + 1) * 512],
                        in_=SG0[:, sq * 512:(sq + 1) * 512],
                        func=AF.Identity, scale=CSB[:, 192 + kb:193 + kb]),
                    r=(f"sgq{sq}", "nw", "bfAw"), w=("bfAw",))
            for tt in range(2):
                for c in range(4):
                    pk = f"ps{4 * tt + c}"
                    for kb in range(32):
                        top(lambda h, kb=kb, c=c, tt=tt: h.matmul(
                                PS[4 * tt + c][:, 0:512],
                                WOB[:, kb * 512 + c * 128:kb * 512 + (c + 1) * 128],
                                ZR[tt][:, kb * 512:(kb + 1) * 512],
                                start=(kb == 0), stop=(kb == 31)),
                            r=("bfAw", ZKEY[tt], pk) if kb == 0
                              else (("bfAw", ZKEY[tt]) if kb == 31 else ("bfAw",)),
                            w=(pk,) if kb in (0, 31) else ())
            for c in range(4):
                oc = eq * 4 + c
                for tt in range(2):
                    aop(lambda h, c=c, tt=tt: h.activation(
                            out=SG1[:, tt * 512:(tt + 1) * 512],
                            in_=PS[4 * tt + c][:, 0:512], func=AF.Identity),
                        r=(f"ps{4 * tt + c}", "sg1"), w=("sg1",))
                dma("scalar", SB["smb"][:, 0:1024], g2_dr[oc, :, :],
                    w=("smb0",), r=(f"g2dr_{oc}", "smb0"))
                vop(lambda h: h.tensor_tensor(out=SG0[:, 0:1024],
                                              in0=SG1[:, 0:1024],
                                              in1=SB["smb"][:, 0:1024],
                                              op=ALU.mult),
                    r=("sg1", "smb0", "sgq0", "sgq1"), w=("sgq0", "sgq1"))
                vop(lambda h, oc=oc: h.tensor_copy(
                        OUT2[:, oc * NTOK:(oc + 1) * NTOK], SG0[:, 0:1024]),
                    r=("sgq0", "sgq1"), w=(f"hi2_{oc}", "bigAfree"))
                vop(lambda h, oc=oc: h.scalar_tensor_tensor(
                        out=SB["smb"][:, 1024:2048],
                        in0=OUT2[:, oc * NTOK:(oc + 1) * NTOK], scalar=-1.0,
                        in1=SG0[:, 0:1024], op0=ALU.mult, op1=ALU.add),
                    r=(f"hi2_{oc}", "sgq0", "sgq1", "smb0"), w=("smb0",))
                dma("scalar", u_dr[oc, :, :], SB["smb"][:, 1024:2048],
                    w=(f"lo2_{oc}", f"udr_{oc}_0", f"udr_{oc}_512"),
                    r=("smb0",))
        P.retag(("sgq0", "sgq1", "sgq2", "sgq3"), ("sg0",))
        HI2 = tuple(f"hi2_{oc}" for oc in range(16))
        LO2 = tuple(f"lo2_{oc}" for oc in range(16))
        P.retag(("bfAw",), ("bfAh2",))
        P.retag(("bfBz",), ("bfBl2",))

        for k in range(16):
            fm2tm(OUT2[:, k * NTOK:(k + 1) * NTOK], SB["bfA"][:], k, f"h2tm_{k}",
                  (f"hi2_{k}",), dst_arena=("bfAh2",))
            dma("sync", SB["smb"][:, 0:1024], u_dr[k, :, :], w=("smb0",),
                r=(f"lo2_{k}", "smb0"))
            fm2tm(SB["smb"][:, 0:1024], SB["bfB"][:], k, f"l2tm_{k}",
                  ("smb0",), dst_arena=("bfBl2",))
        H2TM = tuple(f"h2tm_{k}" for k in range(16))
        L2TM = tuple(f"l2tm_{k}" for k in range(16))
        for i in range(8):
            src = x1_dr.rearrange("(i p) d -> p i d", p=128)[:, i, :]
            dma("scalar", SG0, src, w=("sg0",), r=X1K)
            vop(lambda h, i=i: h.tensor_tensor(out=SG0, in0=SG0,
                                               in1=SB["bfA"][:, i * 2048:(i + 1) * 2048],
                                               op=ALU.add),
                r=("sg0",) + H2TM, w=("sg0",))
            vop(lambda h, i=i: h.tensor_tensor(out=SG0, in0=SG0,
                                               in1=SB["bfB"][:, i * 2048:(i + 1) * 2048],
                                               op=ALU.add),
                r=("sg0",) + L2TM, w=("sg0",))
            dma("gpsimd", out_t.rearrange("(i p) d -> p i d", p=128)[:, i, :], SG0,
                w=(f"xout_{i}",), r=("sg0",))

        final_waits(("gpsimd", "scalar"))

    sems = {}
    names = ([f"E_{e}" for e in Prog.ENGINES]
             + [f"D_{e}_{j}" for e in Prog.ENGINES for j in range(Prog.NDSEM)]
             + ["CC"])
    for n in names:
        sems[n] = stack.enter_context(nc.semaphore(n))
    with nc.Block() as block:
        P.emit(block, sems)
    stack.close()
    return nc


_NC_CACHE = {}


def kernel(**inputs):
    key = "prog"
    if key not in _NC_CACHE:
        _NC_CACHE[key] = build_program()
    nc = _NC_CACHE[key]
    consts = host_constants()
    x = np.ascontiguousarray(inputs["L0_attn_x"], dtype=np.float32)
    bvec = np.stack([inputs["L0_attn_bqk"], inputs["L0_attn_bv"],
                     inputs["L0_attn_bu"], inputs["L0_attn_bo"]]).astype(np.float32)
    lnv = np.stack([inputs["L0_attn_ln_g"], inputs["L0_attn_ln_b"]]).astype(np.float32)
    in_maps = []
    for core in range(8):
        b, r = core // 4, core % 4
        hs = np.zeros((128, 4), np.float32)
        ss = np.zeros((128, 4), np.float32)
        if r > 0:
            hs[:, r - 1] = 1.0
        ss[:, :r] = 1.0
        m = {
            "x": x[b, r * 1024:(r + 1) * 1024, :],
            "Wqk": inputs["L0_attn_Wqk"], "Wv": inputs["L0_attn_Wv"],
            "Wu": inputs["L0_attn_Wu"], "Wo": inputs["L0_attn_Wo"],
            "bvec": bvec, "lnv": lnv,
            "Wg": inputs["L0_ffn_W_g"], "Wu2": inputs["L0_ffn_W_u"],
            "Wd": inputs["L0_ffn_W_d"],
            "Win": inputs["L1_attn_W_in"], "conv_w": inputs["L1_attn_conv_w"],
            "conv_b": inputs["L1_attn_conv_b"], "dtb": inputs["L1_attn_dt_bias"],
            "Av": inputs["L1_attn_A"], "Dv": inputs["L1_attn_D"],
            "nw": inputs["L1_attn_norm_weight"], "Wout": inputs["L1_attn_W_out"],
            "Wf": inputs["L1_ffn_weight"], "Rf": inputs["L1_ffn_rram_variation"],
            "hsel": hs, "ssel": ss,
        }
        m.update(consts)
        in_maps.append({k: np.ascontiguousarray(v) for k, v in m.items()})
    res = run_bass_kernel_spmd(nc, in_maps, core_ids=list(range(8)))
    out = np.empty((2, 4096, 2048), np.float32)
    for core in range(8):
        b, r = core // 4, core % 4
        out[b, r * 1024:(r + 1) * 1024, :] = res.results[core]["out"]
    return out
